# version 02
# baseline (speedup 1.0000x reference)
import contextlib
import os as _os
import numpy as np
import concourse.bass as bass
import concourse.mybir as mybir
from concourse.bass_utils import run_bass_kernel_spmd

F32 = mybir.dt.float32
BF16 = mybir.dt.bfloat16
AF = mybir.ActivationFunctionType
ALU = mybir.AluOpType

E = 1024
FC = 8
I = 4096
IC = 32
NEX = 4
W = 512
HALO = 128
NCORE = 8

IEPS_OUT, IEPS_ATT, IEPS_FFN = 0, 1, 2

_VH = ["MK", "MV", "MR", "TD", "TF", "CMK", "CMR", "FBO",
       "L1G", "L1B", "L2G", "L2B", "FG", "FB"]
_VD = ["EW", "EU", "M1K", "M1V", "M1R", "M1CK", "M1CR", "FG2", "FB2", "ED"]
VG = {n: i for i, n in enumerate(_VH + _VD)}
NVG_HOST = len(_VH)
NVG = len(_VH) + len(_VD)

SC_W1 = 0
SC_WOF = SC_W1 + 32 * 4096
SC_TMK = SC_WOF + 32 * 1024
SC_TMV = SC_TMK + 8 * 1024
SC_TMR = SC_TMV + 8 * 1024
SC_AWO = SC_TMR + 8 * 1024
SC_CMR = SC_AWO + 8 * 1024
SC_CMK = SC_CMR + 8 * 1024
SC_CMV = SC_CMK + 32 * 1024
SC_END = SC_CMV + 32 * 1024


class Sem:
    def __init__(self, ctx, nc, name, eng=None):
        self.h = ctx.enter_context(nc.semaphore(name))
        self.n = 0
        self.eng = eng

    def inc(self, inst, k=1):
        if self.eng is not None:
            inst = self.eng.drain()
        inst.then_inc(self.h, k)
        self.n += k

    def wait(self, eng, n=None):
        eng.wait_ge(self.h, self.n if n is None else n)


class Bank:
    def __init__(self, ap):
        self.ap = ap
        self.readers = []
        self.wn = 0


class DrainProxy:
    _OPS = {"tensor_tensor", "tensor_scalar", "scalar_tensor_tensor", "tensor_copy",
            "reciprocal", "memset", "activation", "tensor_tensor_scan", "copy",
            "tensor_add", "tensor_mul", "tensor_sub", "tensor_relu"}

    def __init__(self, eng):
        self._eng = eng

    def __getattr__(self, name):
        f = getattr(self._eng, name)
        if name in DrainProxy._OPS:
            def wrapped(*a, **k):
                f(*a, **k)
                return self._eng.drain()
            return wrapped
        return f


class B:
    def __init__(self, chunks, debug=False, paranoid=0, ser_pre=False,
                 skip_attn=False, skip_ffn=False, attn_stage=3, real_wk=False, noload=False, sbuf_exp=False, noscan=False, inistage=False, ffn_stage=5, wk_stage=3):
        self.debug = debug
        self.paranoid = paranoid
        self.ser_pre = ser_pre
        self.skip_attn = skip_attn
        self.skip_ffn = skip_ffn
        self.attn_stage = attn_stage
        self.real_wk = real_wk
        self.noload = noload
        self.sbuf_exp = sbuf_exp
        self.noscan = noscan
        self.inistage = inistage
        self.ffn_stage = ffn_stage
        self.wk_stage = wk_stage
        self.chunks = chunks
        self.ntok = sum(chunks)
        self.nown = sum(chunks[1:])
        self.nc = bass.Bass("TRN2", target_bir_lowering=False,
                            detect_race_conditions=False)
        self.ctx = contextlib.ExitStack()

    def declare(self):
        nc, ctx = self.nc, self.ctx
        dt = nc.dram_tensor
        self.x_fm = dt("x_fm", [E, self.ntok], F32, kind="ExternalInput").ap()
        self.mrow = dt("mrow", [1, HALO], F32, kind="ExternalInput").ap()
        self.v128 = dt("v128", [128, 1], F32, kind="ExternalInput").ap()
        self.vecs_in = dt("vecs", [128, 8 * NVG_HOST], F32, kind="ExternalInput").ap()
        self.wr_in = dt("L0_ffn_W_router", [E, NEX], F32, kind="ExternalInput").ap()
        self.b1_in = dt("L0_ffn_b1", [NEX, I], F32, kind="ExternalInput").ap()
        self.w_in = {}
        for name, shape in [("L0_ffn_W1", [NEX, E, I]), ("L0_ffn_Wo", [I, E]),
                            ("L0_attn_tm_Wk", [E, E]), ("L0_attn_tm_Wv", [E, E]),
                            ("L0_attn_tm_Wr", [E, E]), ("L0_attn_Wo", [E, E]),
                            ("L0_attn_cm_Wr", [E, E]), ("L0_attn_cm_Wk", [E, I]),
                            ("L0_attn_cm_Wv", [I, E])]:
            self.w_in[name] = dt(name, shape, F32, kind="ExternalInput").ap()
        self.o_fm = dt("o_fm", [E, self.nown], F32, kind="ExternalOutput").ap()
        if self.debug:
            self.dbg = dt("dbg", [128, 768], F32, kind="ExternalOutput").ap()
        self.wsc = dt("wsc", [128, SC_END], BF16, kind="Internal").ap()

        def sb(name, shape, dtp):
            return ctx.enter_context(nc.sbuf_tensor(name, shape, dtp))[:]
        self.x = sb("xbuf", [128, FC * W], F32)
        self.a = sb("abuf", [128, FC * W], F32)
        self.h = sb("hbuf", [128, FC * W], BF16)
        self.xb = sb("xbbuf", [128, FC * W], BF16)
        self.xm = sb("xmbuf", [128, NEX * FC * W], BF16)
        self.inter = sb("interbuf", [128, IC * W], BF16)
        self.kv = sb("kvbuf", [128, 3 * FC * W], BF16)
        self.tmp1 = sb("tmp1", [128, W], F32)
        self.ones_t = sb("ones_t", [128, W], F32)
        self.ones128 = sb("ones128", [128, 1], F32)
        self.wsl = sb("wsl", [128, 8 * 1024], BF16)
        self.rows = sb("rows", [1, 8 * W], F32)
        self.rows_bf = sb("rows_bf", [1, 2 * W], BF16)
        self.ribits = sb("ribits", [1, 2 * W], mybir.dt.int32)
        self.magic = sb("magic", [1, W], mybir.dt.int32)
        self.maskbf = sb("maskbf", [1, NEX * W], BF16)
        self.ones_row = sb("ones_row", [1, 128], BF16)
        self.tmp4 = sb("tmp4", [4, W], F32)
        self.mask4b = sb("mask4b", [4, W], BF16)
        self.b1b = sb("b1b", [4, I], BF16)
        self.wrouter = sb("wrouter", [128, FC * NEX], F32)
        self.vecs = sb("vecsb", [128, 8 * NVG], F32)
        self.onescol = sb("onescol", [128, 1], BF16)
        self.epsb = sb("epsb", [1, 4], F32)
        self.Sp = sb("Sp", [128, FC], F32)
        self.Sq = sb("Sq", [128, FC], F32)
        self.hc_tm = sb("hc_tm", [128, FC], F32)
        self.hc_cm = sb("hc_cm", [128, FC], F32)
        self.v128s = sb("v128s", [128, 1], F32)
        self.maskb = sb("maskb", [128, HALO], F32)
        self.invmaskb = sb("invmaskb", [128, HALO], F32)
        self.mrow_bf = sb("mrow_bf", [1, HALO], BF16)
        self.inisp = sb("inisp", [128, 1], F32)
        self.inisq = sb("inisq", [128, 1], F32)
        self.dpows = sb("dpows", [128, 10 * FC], F32)
        self.colsave = sb("colsave", [128, 2], F32)
        if self.debug:
            self.dbgs = sb("dbgs", [128, 768], F32)

        if self.real_wk:
            self.gate = sb("rw_gate", [128, FC * W], BF16)
            self.wk_ek = sb("rw_ek", [128, W], F32)
            self.wk_ekv = sb("rw_ekv", [128, W], F32)
            self.wk_p = sb("rw_p", [128, W], F32)
            self.wk_q = sb("rw_q", [128, W], F32)
            self.wk_num = sb("rw_num", [128, W], F32)
            self.wk_den = sb("rw_den", [128, W], F32)
            self.xsq = sb("rw_xsq", [128, FC * W], BF16)
            self.wk_scr = sb("rw_scr", [128, W], F32)
        else:
            xm8 = self.xm.bitcast(mybir.dt.uint8)
            def carve(off_b, n, dtp):
                nb = n * (4 if dtp == F32 else 2)
                return xm8[:, off_b:off_b + nb].bitcast(dtp)
            off = 0
            self.gate = carve(off, FC * W, BF16); off += FC * W * 2
            self.wk_ek = carve(off, W, F32); off += W * 4
            self.wk_ekv = carve(off, W, F32); off += W * 4
            self.wk_p = carve(off, W, F32); off += W * 4
            self.wk_q = carve(off, W, F32); off += W * 4
            self.wk_num = carve(off, W, F32); off += W * 4
            self.wk_den = carve(off, W, F32); off += W * 4
            self.xsq = carve(off, FC * W, BF16); off += FC * W * 2
            self.wk_scr = carve(off, W, F32); off += W * 4
            assert off <= NEX * FC * W * 2
        self.y = self.kv.bitcast(mybir.dt.uint8)[:, 0:FC * W * 4].bitcast(F32)

        self.ps = [Bank(ctx.enter_context(nc.psum_tensor(f"ps{i}", [128, W], F32))[:])
                   for i in range(8)]

        self.T = nc.tensor
        self.V = DrainProxy(nc.vector)
        self.S = DrainProxy(nc.scalar)
        self.G, self.Y = nc.gpsimd, nc.sync

        self.sT = Sem(self.ctx, nc, "sT", self.T)
        self.sV = Sem(self.ctx, nc, "sV", None)
        self.sS = Sem(self.ctx, nc, "sS", None)
        self.sH = Sem(self.ctx, nc, "sH")
        self.sW = Sem(self.ctx, nc, "sW")
        self.sO = Sem(self.ctx, nc, "sO")
        self.slot_free = [0] * 8
        self.slot_loaded = [None] * 8
        self.sSL = [Sem(self.ctx, nc, f"sl{i}") for i in range(8)]
        self.sW1 = [Sem(self.ctx, nc, f"w1_{m}") for m in range(32)]
        self.sPreM = {}
        for nm in ["L0_ffn_Wo", "L0_attn_tm_Wk", "L0_attn_tm_Wv", "L0_attn_tm_Wr",
                   "L0_attn_Wo", "L0_attn_cm_Wr", "L0_attn_cm_Wk", "L0_attn_cm_Wv",
                   "b1"]:
            self.sPreM[nm] = Sem(self.ctx, nc, "pre_" + nm[-6:] + str(len(self.sPreM)))

    def barrier(self, tag=0):
        if not self.paranoid or (self.paranoid != True and self.paranoid != tag):
            pass
        if self.paranoid is True or self.paranoid == tag:
            for eng in (self.T, self.V, self.S):
                for s in (self.sT, self.sV, self.sS):
                    s.wait(eng)

    def fcs(self, buf, fc, L):
        return buf[:, fc * W: fc * W + L]

    def view3(self, buf, L, nfc=FC):
        return buf.rearrange("p (f w) -> p f w", f=nfc)[:, :, 0:L]

    def vg(self, name, fc):
        g = VG[name]
        return self.vecs[:, g * 8 + fc: g * 8 + fc + 1]

    def vgrp(self, name):
        g = VG[name]
        return self.vecs[:, g * 8: g * 8 + 8]

    def row(self, i, L=W):
        return self.rows[0:1, i * W: i * W + L]

    def bank_write(self, bank):
        for (sem, n) in bank.readers:
            self.T.wait_ge(sem.h, n)
        bank.readers = []

    def bank_done(self, bank, inst):
        self.sT.inc(inst)
        bank.wn = self.sT.n

    def read_banks(self, eng, banks):
        for b in banks:
            eng.wait_ge(self.sT.h, b.wn)

    def reads_done(self, banks, sem):
        for b in banks:
            b.readers.append((sem, sem.n))

    def slot_load(self, slots, sc_off, ncols, pre):
        mx = max(self.slot_free[s] for s in slots)
        if mx:
            self.Y.wait_ge(self.sT.h, mx)
        psem, pn = pre
        self.Y.wait_ge(psem.h, pn)
        inst = self.Y.dma_start(out=self.wsl[:, slots[0] * 1024: slots[0] * 1024 + ncols],
                                in_=self.wsc[:, sc_off: sc_off + ncols])
        sl_sem = self.sSL[slots[0]]
        sl_sem.inc(inst, 16)
        for s in slots:
            self.slot_loaded[s] = (sl_sem, sl_sem.n)

    def slot_use(self, slots):
        for s in slots:
            if self.slot_loaded[s] is None:
                continue
            sem, n = self.slot_loaded[s]
            self.T.wait_ge(sem.h, n)

    def slot_release(self, slots):
        for s in slots:
            self.slot_free[s] = self.sT.n

    def slot_tile(self, slot, off, n=128):
        return self.wsl[:, slot * 1024 + off: slot * 1024 + off + n]

    def init_phase(self):
        V, S, T, G, Yq = self.V, self.S, self.T, self.G, self.Y
        V.memset(self.onescol, 1.0)
        V.memset(self.ones_row, 1.0)
        V.memset(self.ones_t, 1.0)
        V.memset(self.ones128, 1.0)
        if self.debug:
            V.memset(self.dbgs, 0.0)
        if self.noload:
            V.memset(self.wsl, 0.01)
        V.memset(self.Sp, 0.0)
        V.memset(self.Sq, 0.0)
        V.memset(self.hc_tm, 0.0)
        V.memset(self.hc_cm, 0.0)
        V.memset(self.epsb[0:1, IEPS_OUT:IEPS_OUT + 1], 1e-5)
        V.memset(self.epsb[0:1, IEPS_ATT:IEPS_ATT + 1], 1e-3)
        V.memset(self.epsb[0:1, IEPS_FFN:IEPS_FFN + 1], 1e-12)
        inst = V.memset(self.magic, 0x5f3759df)
        self.sV.inc(inst)

        self.sH.inc(Yq.dma_start(out=self.vecs[:, 0:8 * NVG_HOST], in_=self.vecs_in), 16)
        self.sH.inc(Yq.dma_start(
            out=self.wrouter, in_=self.wr_in.rearrange("(kc p) e -> p kc e", p=128)), 16)
        self.sH.inc(Yq.dma_start(out=self.v128s, in_=self.v128), 16)
        self.sH.inc(Yq.dma_start(out=self.rows[0:1, 0:HALO], in_=self.mrow), 16)
        self.sPreM["b1"].inc(G.dma_start(out=self.b1b, in_=self.b1_in), 16)
        self.pre_b1 = (self.sPreM["b1"], self.sPreM["b1"].n)

        self.sH.wait(S)
        S.activation(out=self.vgrp("ED"), in_=self.vgrp("TD"), func=AF.Exp)
        S.drain()
        S.activation(out=self.vgrp("EW"), in_=self.vgrp("ED"), func=AF.Exp, scale=-1.0)
        inst = S.activation(out=self.vgrp("EU"), in_=self.vgrp("TF"), func=AF.Exp)
        self.sS.inc(inst)
        self.sH.wait(V)
        for src, dst in [("MK", "M1K"), ("MV", "M1V"), ("MR", "M1R"),
                         ("CMK", "M1CK"), ("CMR", "M1CR")]:
            V.tensor_scalar(out=self.vgrp(dst), in0=self.vgrp(src), scalar1=-1.0,
                            scalar2=1.0, op0=ALU.mult, op1=ALU.add)
        V.tensor_scalar(out=self.vgrp("FG2"), in0=self.vgrp("FG"), scalar1=0.5,
                        scalar2=None, op0=ALU.mult)
        V.tensor_scalar(out=self.vgrp("FB2"), in0=self.vgrp("FB"), scalar1=0.5,
                        scalar2=None, op0=ALU.mult)
        for s in range(10):
            inst = S.activation(out=self.dpows[:, s * FC:(s + 1) * FC],
                                in_=self.vgrp("ED"), func=AF.Exp, scale=-float(2 ** s))
        self.sS.inc(inst)
        inst = V.tensor_copy(out=self.mrow_bf, in_=self.rows[0:1, 0:HALO])
        self.sV.inc(inst)
        bk = self.ps[0]
        self.bank_write(bk)
        self.sV.wait(T)
        inst = T.matmul(bk.ap[:, 0:HALO], self.ones_row, self.mrow_bf, start=True, stop=True)
        self.bank_done(bk, inst)
        self.read_banks(V, [bk])
        V.tensor_copy(out=self.maskb, in_=bk.ap[:, 0:HALO])
        inst = V.tensor_scalar(out=self.invmaskb, in0=bk.ap[:, 0:HALO], scalar1=-1.0,
                               scalar2=1.0, op0=ALU.mult, op1=ALU.add)
        self.sV.inc(inst)
        self.reads_done([bk], self.sV)
        self.init_sH = self.sH.n

    def prepass(self):
        G = self.G
        self.pre = {}
        w1 = self.w_in["L0_ffn_W1"]
        for m in range(32):
            if self.ser_pre and m > 0:
                G.wait_ge(self.sW1[m - 1].h, self.sW1[m - 1].n)
            src = w1[:, :, m * 128:(m + 1) * 128].rearrange("e (kc p) j -> p e kc j", p=128)
            dst = self.wsc[:, SC_W1 + m * 4096: SC_W1 + (m + 1) * 4096]
            dst = dst.rearrange("p (e kc j) -> p e kc j", e=NEX, kc=FC)
            self.sW1[m].inc(G.dma_start(out=dst, in_=src), 16)
            self.pre[("W1", m)] = (self.sW1[m], self.sW1[m].n)
            self.last_pre = (self.sW1[m], self.sW1[m].n)

        def kslab(name, off, nk):
            sem = self.sPreM[name]
            if self.ser_pre:
                G.wait_ge(self.last_pre[0].h, self.last_pre[1])
            src = self.w_in[name].rearrange("(kc p) j -> p kc j", p=128)
            dst = self.wsc[:, off: off + nk * 1024].rearrange("p (kc j) -> p kc j", kc=nk)
            sem.inc(G.dma_start(out=dst, in_=src), 16)
            self.pre[name] = (sem, sem.n)
            self.last_pre = (sem, sem.n)

        def mslab(name, off, nm):
            sem = self.sPreM[name]
            for m in range(nm):
                if self.ser_pre:
                    G.wait_ge(self.last_pre[0].h, self.last_pre[1])
                src = self.w_in[name][:, m * 128:(m + 1) * 128]
                src = src.rearrange("(kc p) j -> p kc j", p=128)
                dst = self.wsc[:, off + m * 1024: off + (m + 1) * 1024]
                dst = dst.rearrange("p (kc j) -> p kc j", kc=FC)
                sem.inc(G.dma_start(out=dst, in_=src), 16)
                self.last_pre = (sem, sem.n)
            self.pre[name] = (sem, sem.n)

        kslab("L0_ffn_Wo", SC_WOF, 32)
        mslab("L0_attn_tm_Wk", SC_TMK, 8)
        mslab("L0_attn_tm_Wv", SC_TMV, 8)
        mslab("L0_attn_tm_Wr", SC_TMR, 8)
        mslab("L0_attn_Wo", SC_AWO, 8)
        mslab("L0_attn_cm_Wr", SC_CMR, 8)
        mslab("L0_attn_cm_Wk", SC_CMK, 32)
        kslab("L0_attn_cm_Wv", SC_CMV, 32)

    def layer_norm(self, src, L, eps_idx):
        T, V, S = self.T, self.V, self.S
        self.sS.wait(V)
        inst = V.tensor_copy(out=self.view3(self.xb, L), in_=self.view3(src, L))
        self.sV.inc(inst)
        inst = V.tensor_tensor(out=self.view3(self.xsq, L), in0=self.view3(self.xb, L),
                               in1=self.view3(self.xb, L), op=ALU.mult)
        self.sV.inc(inst)
        self.barrier(1)
        psx, psq = self.ps[5], self.ps[6]
        self.bank_write(psx)
        self.bank_write(psq)
        self.sV.wait(T)
        for kc in range(FC):
            inst = T.matmul(psx.ap[0:1, 0:L], self.onescol, self.fcs(self.xb, kc, L),
                            start=(kc == 0), stop=(kc == FC - 1))
        self.bank_done(psx, inst)
        for kc in range(FC):
            inst = T.matmul(psq.ap[0:1, 0:L], self.onescol, self.fcs(self.xsq, kc, L),
                            start=(kc == 0), stop=(kc == FC - 1))
        self.bank_done(psq, inst)
        self.barrier(2)
        MU, MS, VAR, NMR = 0, 1, 2, 3
        self.read_banks(V, [psx, psq])
        V.tensor_scalar(out=self.row(MU, L), in0=psx.ap[0:1, 0:L], scalar1=1.0 / E,
                        scalar2=None, op0=ALU.mult)
        inst = V.tensor_scalar(out=self.row(MS, L), in0=psq.ap[0:1, 0:L], scalar1=1.0 / E,
                               scalar2=None, op0=ALU.mult)
        self.sV.inc(inst)
        self.reads_done([psx, psq], self.sV)
        V.tensor_tensor(out=self.row(VAR, L), in0=self.row(MU, L), in1=self.row(MU, L),
                        op=ALU.mult)
        V.tensor_tensor(out=self.row(VAR, L), in0=self.row(MS, L),
                        in1=self.row(VAR, L), op=ALU.subtract)
        eps_sc = self.epsb[0:1, eps_idx:eps_idx + 1]
        vp = self.row(7, L)
        V.tensor_scalar(out=vp, in0=self.row(VAR, L), scalar1=eps_sc, scalar2=None,
                        op0=ALU.add)
        V.drain()
        V.tensor_copy(out=self.ribits.bitcast(F32)[0:1, 0:L], in_=vp)
        V.drain()
        V.tensor_scalar(out=self.ribits[0:1, 0:L], in0=self.ribits[0:1, 0:L],
                        scalar1=1, scalar2=None, op0=ALU.arith_shift_right)
        V.drain()
        V.tensor_tensor(out=self.ribits[0:1, 0:L], in0=self.magic[0:1, 0:L],
                        in1=self.ribits[0:1, 0:L], op=ALU.subtract)
        V.drain()
        y = self.row(VAR, L)
        V.tensor_copy(out=y, in_=self.ribits.bitcast(F32)[0:1, 0:L])
        V.drain()
        half = self.row(6, L)
        for _ in range(3):
            V.tensor_tensor(out=half, in0=y, in1=y, op=ALU.mult)
            V.drain()
            V.tensor_tensor(out=half, in0=half, in1=vp, op=ALU.mult)
            V.drain()
            V.tensor_scalar(out=half, in0=half, scalar1=-0.5, scalar2=1.5,
                            op0=ALU.mult, op1=ALU.add)
            V.drain()
            V.tensor_tensor(out=y, in0=y, in1=half, op=ALU.mult)
            V.drain()
        V.tensor_copy(out=self.rows_bf[0:1, 0:L], in_=self.row(VAR, L))
        V.tensor_tensor(out=self.row(NMR, L), in0=self.row(MU, L), in1=self.row(VAR, L),
                        op=ALU.mult)
        inst = V.tensor_scalar(out=self.rows_bf[0:1, W:W + L], in0=self.row(NMR, L),
                               scalar1=-1.0, scalar2=None, op0=ALU.mult)
        self.sV.inc(inst)
        self.barrier(3)
        pA, pB = self.ps[3], self.ps[4]
        self.bank_write(pA)
        self.bank_write(pB)
        self.sV.wait(T)
        inst = T.matmul(pA.ap[:, 0:L], self.ones_row, self.rows_bf[0:1, 0:L],
                        start=True, stop=True)
        self.bank_done(pA, inst)
        inst = T.matmul(pB.ap[:, 0:L], self.ones_row, self.rows_bf[0:1, W:W + L],
                        start=True, stop=True)
        self.bank_done(pB, inst)
        return pA, pB

    def normalize(self, src, dst, L, pA, pB, g=None, b=None):
        V = self.V
        self.barrier(4)
        self.read_banks(V, [pA, pB])
        for fc in range(FC):
            V.tensor_tensor(out=self.tmp1[:, 0:L], in0=self.fcs(src, fc, L),
                            in1=pA.ap[:, 0:L], op=ALU.mult)
            if g is None:
                inst = V.tensor_tensor(out=self.fcs(dst, fc, L), in0=self.tmp1[:, 0:L],
                                       in1=pB.ap[:, 0:L], op=ALU.add)
            else:
                V.tensor_tensor(out=self.tmp1[:, 0:L], in0=self.tmp1[:, 0:L],
                                in1=pB.ap[:, 0:L], op=ALU.add)
                inst = V.tensor_scalar(out=self.fcs(dst, fc, L), in0=self.tmp1[:, 0:L],
                                       scalar1=self.vg(g, fc), scalar2=self.vg(b, fc),
                                       op0=ALU.mult, op1=ALU.add)
        self.sV.inc(inst)
        self.reads_done([pA, pB], self.sV)

    def ffn(self, c, L):
        T, V, S, Yq = self.T, self.V, self.S, self.Y
        pA, pB = self.layer_norm(self.x, L, IEPS_OUT)
        self.normalize(self.x, self.a, L, pA, pB)
        if self.debug and c == 0 and not getattr(self, '_dumped_ffn', False):
            self.dbg_dump("ffn_a", 64, self.a, L)
        inst = V.tensor_copy(out=self.view3(self.xb, L), in_=self.view3(self.a, L))
        self.sV.inc(inst)
        if self.ffn_stage <= 1:
            return

        pR = self.ps[2]
        self.bank_write(pR)
        self.sV.wait(T)
        T.wait_ge(self.sH.h, self.init_sH)
        for kc in range(FC):
            inst = T.matmul(pR.ap[0:4, 0:L], self.wrouter[:, kc * 4:(kc + 1) * 4],
                            self.fcs(self.a, kc, L), start=(kc == 0), stop=(kc == FC - 1))
        self.bank_done(pR, inst)
        self.read_banks(V, [pR])
        inst = V.tensor_copy(out=self.tmp4[0:4, 0:L], in_=pR.ap[0:4, 0:L])
        self.sV.inc(inst)
        self.reads_done([pR], self.sV)
        self.sV.wait(Yq)
        for e in range(NEX):
            self.sH.inc(Yq.dma_start(out=self.rows[0:1, (4 + e) * W:(4 + e) * W + L],
                                     in_=self.tmp4[e:e + 1, 0:L]), 16)
        LG = lambda e: self.row(4 + e, L)
        self.sH.wait(V)
        V.tensor_tensor(out=self.row(0, L), in0=LG(0), in1=LG(1), op=ALU.max)
        V.tensor_tensor(out=self.row(1, L), in0=LG(2), in1=LG(3), op=ALU.max)
        V.tensor_tensor(out=self.row(0, L), in0=self.row(0, L), in1=self.row(1, L),
                        op=ALU.max)
        V.tensor_tensor(out=LG(0), in0=LG(0), in1=self.row(0, L), op=ALU.is_ge)
        V.tensor_tensor(out=LG(1), in0=LG(1), in1=self.row(0, L), op=ALU.is_ge)
        V.tensor_tensor(out=LG(2), in0=LG(2), in1=self.row(0, L), op=ALU.is_ge)
        V.tensor_scalar(out=self.row(1, L), in0=LG(0), scalar1=-1.0, scalar2=1.0,
                        op0=ALU.mult, op1=ALU.add)
        V.tensor_tensor(out=LG(1), in0=LG(1), in1=self.row(1, L), op=ALU.mult)
        V.tensor_tensor(out=self.row(2, L), in0=LG(0), in1=LG(1), op=ALU.add)
        V.tensor_scalar(out=self.row(1, L), in0=self.row(2, L), scalar1=-1.0, scalar2=1.0,
                        op0=ALU.mult, op1=ALU.add)
        V.tensor_tensor(out=LG(2), in0=LG(2), in1=self.row(1, L), op=ALU.mult)
        V.tensor_tensor(out=self.row(2, L), in0=self.row(2, L), in1=LG(2), op=ALU.add)
        V.tensor_scalar(out=LG(3), in0=self.row(2, L), scalar1=-1.0, scalar2=1.0,
                        op0=ALU.mult, op1=ALU.add)
        for e in range(NEX):
            inst = V.tensor_copy(out=self.maskbf[0:1, e * W: e * W + L], in_=LG(e))
        self.sV.inc(inst)
        self.sV.wait(Yq)
        for e in range(NEX):
            self.sH.inc(Yq.dma_start(out=self.tmp4[e:e + 1, 0:L],
                                     in_=self.rows[0:1, (4 + e) * W:(4 + e) * W + L]), 16)
        self.sH.wait(V)
        inst = V.tensor_copy(out=self.mask4b[0:4, 0:L], in_=self.tmp4[0:4, 0:L])
        self.sV.inc(inst)

        for e in range(NEX):
            bk = self.ps[3 + (e % 2)]
            self.bank_write(bk)
            self.sV.wait(T)
            inst = T.matmul(bk.ap[:, 0:L], self.ones_row,
                            self.maskbf[0:1, e * W: e * W + L], start=True, stop=True)
            self.bank_done(bk, inst)
            self.read_banks(V, [bk])
            for kc in range(FC):
                inst = V.tensor_tensor(
                    out=self.xm[:, (e * FC + kc) * W: (e * FC + kc) * W + L],
                    in0=self.fcs(self.xb, kc, L), in1=bk.ap[:, 0:L], op=ALU.mult)
            self.sV.inc(inst)
            self.reads_done([bk], self.sV)

        if self.ffn_stage <= 2:
            return
        self.barrier(5)
        w1slots = [[0, 1, 2, 3], [4, 5, 6, 7]]
        self.slot_load(w1slots[0], SC_W1, 4096, self.pre[("W1", 0)])
        self.slot_load(w1slots[1], SC_W1 + 4096, 4096, self.pre[("W1", 1)])
        T.wait_ge(self.pre_b1[0].h, self.pre_b1[1])
        for m in range(IC):
            sl = w1slots[m % 2]
            bk = self.ps[m % 2]
            self.bank_write(bk)
            self.slot_use(sl)
            self.sV.wait(T)
            for e in range(NEX):
                for kc in range(FC):
                    T.matmul(bk.ap[:, 0:L], self.slot_tile(sl[0], (e * FC + kc) * 128),
                             self.xm[:, (e * FC + kc) * W: (e * FC + kc) * W + L],
                             start=(e == 0 and kc == 0), stop=False)
            inst = T.matmul(bk.ap[:, 0:L], self.b1b[0:4, m * 128:(m + 1) * 128],
                            self.mask4b[0:4, 0:L], start=False, stop=True)
            self.bank_done(bk, inst)
            self.slot_release(sl)
            if m + 2 < IC:
                self.slot_load(sl, SC_W1 + (m + 2) * 4096, 4096, self.pre[("W1", m + 2)])
            self.read_banks(S, [bk])
            inst = S.activation(out=self.inter[:, m * W: m * W + L], in_=bk.ap[:, 0:L],
                                func=AF.Gelu)
            self.sS.inc(inst)
            self.reads_done([bk], self.sS)

        if self.ffn_stage <= 3:
            return
        self.barrier(6)
        for i in range(4):
            self.slot_load([i], SC_WOF + i * 1024, 1024, self.pre["L0_ffn_Wo"])
        for bk in self.ps:
            self.bank_write(bk)
        self.sS.wait(T)
        for kc in range(IC):
            sl = kc % 4
            self.slot_use([sl])
            for m in range(FC):
                inst = T.matmul(self.ps[m].ap[:, 0:L], self.slot_tile(sl, m * 128),
                                self.inter[:, kc * W: kc * W + L],
                                start=(kc == 0), stop=(kc == IC - 1))
            self.sT.inc(inst)
            self.slot_release([sl])
            if kc + 4 < IC:
                self.slot_load([sl], SC_WOF + (kc + 4) * 1024, 1024, self.pre["L0_ffn_Wo"])
        self.sT.inc(inst)
        for bk in self.ps:
            bk.wn = self.sT.n
        self.read_banks(V, self.ps)
        for m in range(FC):
            inst = V.scalar_tensor_tensor(out=self.fcs(self.y, m, L),
                                          in0=self.ps[m].ap[:, 0:L],
                                          scalar=self.vg("FBO", m),
                                          in1=self.fcs(self.a, m, L),
                                          op0=ALU.add, op1=ALU.add)
        self.sV.inc(inst)
        self.reads_done(self.ps, self.sV)

        if self.debug and c == 0 and not getattr(self, '_dumped_ffn', False):
            self.dbg_dump("ffn_y", 72, self.y, L)
            self._dumped_ffn = True
        pA, pB = self.layer_norm(self.y, L, IEPS_FFN)
        self.read_banks(V, [pA, pB])
        for fc in range(FC):
            V.tensor_tensor(out=self.tmp1[:, 0:L], in0=self.fcs(self.y, fc, L),
                            in1=pA.ap[:, 0:L], op=ALU.mult)
            V.tensor_tensor(out=self.tmp1[:, 0:L], in0=self.tmp1[:, 0:L],
                            in1=pB.ap[:, 0:L], op=ALU.add)
            V.tensor_scalar(out=self.tmp1[:, 0:L], in0=self.tmp1[:, 0:L],
                            scalar1=self.vg("FG2", fc), scalar2=self.vg("FB2", fc),
                            op0=ALU.mult, op1=ALU.add)
            inst = V.tensor_tensor(out=self.fcs(self.x, fc, L), in0=self.tmp1[:, 0:L],
                                   in1=self.fcs(self.x, fc, L), op=ALU.add)
        self.sV.inc(inst)
        self.reads_done([pA, pB], self.sV)

    def mixes(self, L, specs, carry):
        V = self.V
        for (mx, m1, dst_off) in specs:
            for fc in range(FC):
                h_ = self.fcs(self.h, fc, L)
                hprev = self.h[:, fc * W: fc * W + L - 1]
                dst = self.kv[:, dst_off + fc * W: dst_off + fc * W + L]
                V.tensor_scalar(out=self.tmp1[:, 0:L], in0=h_, scalar1=self.vg(mx, fc),
                                scalar2=None, op0=ALU.mult)
                V.scalar_tensor_tensor(out=dst[:, 0:1], in0=carry[:, fc:fc + 1],
                                       scalar=self.vg(m1, fc), in1=self.tmp1[:, 0:1],
                                       op0=ALU.mult, op1=ALU.add)
                inst = V.scalar_tensor_tensor(out=dst[:, 1:L], in0=hprev,
                                              scalar=self.vg(m1, fc),
                                              in1=self.tmp1[:, 1:L],
                                              op0=ALU.mult, op1=ALU.add)
        self.sV.inc(inst)
        for fc in range(FC):
            inst = V.tensor_copy(out=carry[:, fc:fc + 1],
                                 in_=self.h[:, fc * W + L - 1: fc * W + L])
        self.sV.inc(inst)

    def attn(self, c, L):
        T, V, S = self.T, self.V, self.S
        pA, pB = self.layer_norm(self.x, L, IEPS_OUT)
        self.normalize(self.x, self.a, L, pA, pB)
        if c == 0 and self.debug:
            self.dbg_dump("a0", 48, self.a, L)
        pA, pB = self.layer_norm(self.a, L, IEPS_ATT)
        self.normalize(self.a, self.h, L, pA, pB, g="L1G", b="L1B")
        if c == 0 and self.debug:
            inst = self.V.tensor_copy(
                out=self.dbgs.rearrange("p (f o) -> p f o", o=1)[:, 56:56 + FC, :],
                in_=self.h.rearrange("p (f w) -> p f w", f=FC)[:, :, L - 1:L])
            self.sV.inc(inst)
        self.mixes(L, [("MK", "M1K", 0), ("MV", "M1V", FC * W), ("MR", "M1R", 2 * FC * W)],
                   self.hc_tm)
        if c == 0:
            inst = V.tensor_scalar(out=self.hc_tm, in0=self.hc_tm, scalar1=self.v128s,
                                   scalar2=None, op0=ALU.mult)
            self.sV.inc(inst)
        if self.debug and c == 0:
            inst = V.tensor_copy(out=self.dbgs[:, 553:553 + 64], in_=self.kv[:, 7 * W:7 * W + 64])
            self.sV.inc(inst)
        if self.attn_stage <= 1:
            for fc in range(FC):
                inst = V.tensor_tensor(out=self.fcs(self.x, fc, L),
                                       in0=self.fcs(self.a, fc, L),
                                       in1=self.fcs(self.x, fc, L), op=ALU.add)
            self.sV.inc(inst)
            return

        self.barrier(7)
        mats = [("L0_attn_tm_Wk", SC_TMK, 0), ("L0_attn_tm_Wv", SC_TMV, FC * W),
                ("L0_attn_tm_Wr", SC_TMR, 2 * FC * W)]
        for i, (mat, off, _) in enumerate(mats):
            if not self.noload:
                self.slot_load([i], off, 1024, self.pre[mat])
        for j in range(FC):
            base = 0 if j % 2 == 0 else 3
            bks = [self.ps[base + i] for i in range(3)]
            for i, (mat, off, dst_off) in enumerate(mats):
                bk = bks[i]
                sl = base + i
                self.bank_write(bk)
                self.slot_use([sl])
                self.sV.wait(T)
                for kc in range(FC):
                    inst = T.matmul(bk.ap[:, 0:L], self.slot_tile(sl, kc * 128),
                                    self.kv[:, dst_off + kc * W: dst_off + kc * W + L],
                                    start=(kc == 0), stop=(kc == FC - 1))
                self.bank_done(bk, inst)
                self.slot_release([sl])
                if j + 1 < FC and not self.noload:
                    self.slot_load([(3 - base) + i], off + (j + 1) * 1024, 1024,
                                   self.pre[mat])
            kb, vb, rb = bks
            self.read_banks(S, [kb, rb])
            S.activation(out=self.wk_ek[:, 0:L], in_=kb.ap[:, 0:L], func=AF.Exp)
            inst = S.activation(out=self.wk_num[:, 0:L], in_=rb.ap[:, 0:L], func=AF.Exp,
                                scale=-1.0)
            self.sS.inc(inst)
            self.reads_done([kb, rb], self.sS)
            self.sS.wait(V)
            if c == 0:
                V.tensor_tensor(out=self.wk_ek[:, 0:L], in0=self.wk_ek[:, 0:L],
                                in1=self.maskb[:, 0:L], op=ALU.mult)
            self.read_banks(V, [vb])
            inst = V.tensor_tensor(out=self.wk_ekv[:, 0:L], in0=self.wk_ek[:, 0:L],
                                   in1=vb.ap[:, 0:L], op=ALU.mult)
            self.sV.inc(inst)
            self.reads_done([vb], self.sV)
            if self.wk_stage <= 1:
                inst = V.tensor_copy(out=self.gate[:, j * W: j * W + L],
                                     in_=self.wk_ekv[:, 0:L])
                self.sV.inc(inst)
                continue
            V.tensor_copy(out=self.colsave[:, 0:1], in_=self.wk_ekv[:, 0:1])
            V.tensor_copy(out=self.colsave[:, 1:2], in_=self.wk_ek[:, 0:1])
            V.scalar_tensor_tensor(out=self.wk_ekv[:, 0:1], in0=self.Sp[:, j:j + 1],
                                   scalar=self.vg("EW", j), in1=self.wk_ekv[:, 0:1],
                                   op0=ALU.mult, op1=ALU.add)
            V.scalar_tensor_tensor(out=self.wk_ek[:, 0:1], in0=self.Sq[:, j:j + 1],
                                   scalar=self.vg("EW", j), in1=self.wk_ek[:, 0:1],
                                   op0=ALU.mult, op1=ALU.add)
            V.drain()
            pbuf, pfree = self.prefix(self.wk_ekv, self.wk_p, self.wk_scr, L, j)
            V.drain()
            qbuf, qfree = self.prefix(self.wk_ek, self.wk_q, pfree, L, j)
            V.drain()
            if self.wk_stage <= 2:
                V.drain()
                V.tensor_copy(out=self.Sp[:, j:j + 1], in_=pbuf[:, L - 1:L])
                V.tensor_copy(out=self.Sq[:, j:j + 1], in_=qbuf[:, L - 1:L])
                inst = V.tensor_copy(out=self.gate[:, j * W: j * W + L],
                                     in_=pbuf[:, 0:L])
                self.sV.inc(inst)
                continue
            V.scalar_tensor_tensor(out=self.wk_den[:, 0:1], in0=self.colsave[:, 1:2],
                                   scalar=self.vg("EU", j), in1=self.Sq[:, j:j + 1],
                                   op0=ALU.mult, op1=ALU.add)
            V.scalar_tensor_tensor(out=self.wk_den[:, 1:L], in0=self.wk_ek[:, 1:L],
                                   scalar=self.vg("EU", j), in1=qbuf[:, 0:L - 1],
                                   op0=ALU.mult, op1=ALU.add)
            V.drain()
            if c == 0:
                V.tensor_tensor(out=self.wk_den[:, 0:L], in0=self.wk_den[:, 0:L],
                                in1=self.invmaskb[:, 0:L], op=ALU.add)
            V.scalar_tensor_tensor(out=self.wk_den[:, 0:L], in0=self.wk_num[:, 0:L],
                                   scalar=1.0, in1=self.wk_den[:, 0:L],
                                   op0=ALU.add, op1=ALU.mult)
            V.drain()
            V.scalar_tensor_tensor(out=self.wk_ekv[:, 0:1], in0=self.colsave[:, 0:1],
                                   scalar=self.vg("EU", j), in1=self.Sp[:, j:j + 1],
                                   op0=ALU.mult, op1=ALU.add)
            V.scalar_tensor_tensor(out=self.wk_ekv[:, 1:L], in0=self.wk_ekv[:, 1:L],
                                   scalar=self.vg("EU", j), in1=pbuf[:, 0:L - 1],
                                   op0=ALU.mult, op1=ALU.add)
            V.drain()
            V.tensor_copy(out=self.Sp[:, j:j + 1], in_=pbuf[:, L - 1:L])
            inst = V.tensor_copy(out=self.Sq[:, j:j + 1], in_=qbuf[:, L - 1:L])
            self.sV.inc(inst)
            if self.debug and c == 1 and j in (1, 7):
                base_d = 208 if j == 1 else 240
                V.tensor_copy(out=self.dbgs[:, base_d:base_d + 16], in_=self.wk_ek[:, 0:16])
                inst = V.tensor_copy(out=self.dbgs[:, base_d + 16:base_d + 32],
                                     in_=pbuf[:, 0:16])
                self.sV.inc(inst)
            if self.debug and c == 1 and j == 0:
                V.tensor_copy(out=self.dbgs[:, 304:320], in_=self.wsl[:, 0:16])
                self.read_banks(V, [kb])
                V.tensor_copy(out=self.dbgs[:, 320:336], in_=kb.ap[:, 0:16])
                self.reads_done([kb], self.sV)
                V.tensor_copy(out=self.dbgs[:, 80:96], in_=self.wk_ek[:, 0:16])
                V.tensor_copy(out=self.dbgs[:, 96:112], in_=self.wk_ekv[:, 0:16])
                V.tensor_copy(out=self.dbgs[:, 112:128], in_=self.kv[:, 0:16])
                V.tensor_copy(out=self.dbgs[:, 128:144], in_=self.kv[:, FC * W:FC * W + 16])
                V.tensor_copy(out=self.dbgs[:, 144:160].bitcast(BF16)[:, 0:16],
                              in_=self.h[:, 0:16])
                V.tensor_copy(out=self.dbgs[:, 176:192], in_=pbuf[:, 0:16])
                inst = V.tensor_copy(out=self.dbgs[:, 192:208], in_=qbuf[:, 0:16])
                self.sV.inc(inst)
            rdst = qfree
            V.reciprocal(out=rdst[:, 0:L], in_=self.wk_den[:, 0:L])
            V.drain()
            inst = V.tensor_tensor(out=self.gate[:, j * W: j * W + L],
                                   in0=self.wk_ekv[:, 0:L], in1=rdst[:, 0:L],
                                   op=ALU.mult)
            self.sV.inc(inst)
            if self.debug and c == 0 and j == 3:
                inst = V.tensor_copy(out=self.dbgs[:, 617:617 + 64],
                                     in_=self.kv[:, 7 * W:7 * W + 64])
                self.sV.inc(inst)

        if self.debug and c == 0:
            V.tensor_copy(out=self.dbgs[:, 80:208], in_=self.kv[:, 7 * W:7 * W + 128])
            V.tensor_copy(out=self.dbgs[:, 208:336], in_=self.wk_ek[:, 0:128])
            inst = V.tensor_copy(out=self.dbgs[:, 425:553],
                                 in_=self.wsl[:, 3 * 1024:3 * 1024 + 128])
            self.sV.inc(inst)
        if not self.noload:
            self.slot_load([6], SC_AWO, 1024, self.pre["L0_attn_Wo"])
            self.slot_load([7], SC_AWO + 1024, 1024, self.pre["L0_attn_Wo"])
        for m in range(FC):
            sl = 6 + (m % 2)
            bk = self.ps[6 + (m % 2)]
            self.bank_write(bk)
            self.slot_use([sl])
            self.sV.wait(T)
            for kc in range(FC):
                inst = T.matmul(bk.ap[:, 0:L], self.slot_tile(sl, kc * 128),
                                self.gate[:, kc * W: kc * W + L],
                                start=(kc == 0), stop=(kc == FC - 1))
            self.bank_done(bk, inst)
            self.slot_release([sl])
            if m + 2 < FC and not self.noload:
                self.slot_load([sl], SC_AWO + (m + 2) * 1024, 1024, self.pre["L0_attn_Wo"])
            self.read_banks(V, [bk])
            inst = V.tensor_tensor(out=self.fcs(self.a, m, L), in0=bk.ap[:, 0:L],
                                   in1=self.fcs(self.a, m, L), op=ALU.add)
            self.sV.inc(inst)
            self.reads_done([bk], self.sV)

        if self.attn_stage <= 2:
            if self.debug and c == 1:
                inst = V.tensor_copy(out=self.dbgs[:, 272:288],
                                     in_=self.fcs(self.a, 0, L)[:, 0:16])
                self.sV.inc(inst)
            for fc in range(FC):
                inst = V.tensor_tensor(out=self.fcs(self.x, fc, L),
                                       in0=self.fcs(self.a, fc, L),
                                       in1=self.fcs(self.x, fc, L), op=ALU.add)
            self.sV.inc(inst)
            if self.debug and c == 1:
                inst = V.tensor_copy(out=self.dbgs[:, 288:304],
                                     in_=self.fcs(self.x, 0, L)[:, 0:16])
                self.sV.inc(inst)
            return
        if self.debug and c == 1:
            inst = V.tensor_copy(out=self.dbgs[:, 272:288], in_=self.fcs(self.a, 0, L)[:, 0:16])
            self.sV.inc(inst)
        pA, pB = self.layer_norm(self.a, L, IEPS_ATT)
        self.normalize(self.a, self.h, L, pA, pB, g="L2G", b="L2B")
        self.mixes(L, [("CMK", "M1CK", 0), ("CMR", "M1CR", 2 * FC * W)], self.hc_cm)
        if c == 0:
            inst = V.tensor_scalar(out=self.hc_cm, in0=self.hc_cm, scalar1=self.v128s,
                                   scalar2=None, op0=ALU.mult)
            self.sV.inc(inst)

        self.slot_load([6], SC_CMR, 1024, self.pre["L0_attn_cm_Wr"])
        self.slot_load([7], SC_CMR + 1024, 1024, self.pre["L0_attn_cm_Wr"])
        for m in range(FC):
            sl = 6 + (m % 2)
            bk = self.ps[6 + (m % 2)]
            self.bank_write(bk)
            self.slot_use([sl])
            self.sV.wait(T)
            for kc in range(FC):
                inst = T.matmul(bk.ap[:, 0:L], self.slot_tile(sl, kc * 128),
                                self.kv[:, 2 * FC * W + kc * W: 2 * FC * W + kc * W + L],
                                start=(kc == 0), stop=(kc == FC - 1))
            self.bank_done(bk, inst)
            self.slot_release([sl])
            if m + 2 < FC:
                self.slot_load([sl], SC_CMR + (m + 2) * 1024, 1024,
                               self.pre["L0_attn_cm_Wr"])
            self.read_banks(S, [bk])
            inst = S.activation(out=self.tmp1[:, 0:L], in_=bk.ap[:, 0:L], func=AF.Exp,
                                scale=-1.0)
            self.sS.inc(inst)
            self.reads_done([bk], self.sS)
            self.sS.wait(V)
            V.tensor_scalar(out=self.tmp1[:, 0:L], in0=self.tmp1[:, 0:L], scalar1=1.0,
                            scalar2=None, op0=ALU.add)
            V.drain()
            V.reciprocal(out=self.wk_scr[:, 0:L], in_=self.tmp1[:, 0:L])
            V.drain()
            inst = V.tensor_copy(out=self.gate[:, m * W: m * W + L], in_=self.wk_scr[:, 0:L])
            self.sV.inc(inst)

        self.slot_load([6], SC_CMK, 1024, self.pre["L0_attn_cm_Wk"])
        self.slot_load([7], SC_CMK + 1024, 1024, self.pre["L0_attn_cm_Wk"])
        for m in range(IC):
            sl = 6 + (m % 2)
            bk = self.ps[6 + (m % 2)]
            self.bank_write(bk)
            self.slot_use([sl])
            self.sV.wait(T)
            for kc in range(FC):
                inst = T.matmul(bk.ap[:, 0:L], self.slot_tile(sl, kc * 128),
                                self.kv[:, kc * W: kc * W + L],
                                start=(kc == 0), stop=(kc == FC - 1))
            self.bank_done(bk, inst)
            self.slot_release([sl])
            if m + 2 < IC:
                self.slot_load([sl], SC_CMK + (m + 2) * 1024, 1024,
                               self.pre["L0_attn_cm_Wk"])
            self.read_banks(V, [bk])
            V.tensor_scalar(out=self.tmp1[:, 0:L], in0=bk.ap[:, 0:L], scalar1=0.0,
                            scalar2=None, op0=ALU.max)
            inst = V.tensor_tensor(out=self.inter[:, m * W: m * W + L],
                                   in0=self.tmp1[:, 0:L], in1=bk.ap[:, 0:L], op=ALU.mult)
            self.sV.inc(inst)
            self.reads_done([bk], self.sV)

        for fc in range(FC):
            inst = V.tensor_tensor(out=self.fcs(self.x, fc, L), in0=self.fcs(self.a, fc, L),
                                   in1=self.fcs(self.x, fc, L), op=ALU.add)
        self.sV.inc(inst)

        for i in range(4):
            self.slot_load([i], SC_CMV + i * 1024, 1024, self.pre["L0_attn_cm_Wv"])
        for bk in self.ps:
            self.bank_write(bk)
        self.sS.wait(T)
        for kc in range(IC):
            sl = kc % 4
            self.slot_use([sl])
            for m in range(FC):
                inst = T.matmul(self.ps[m].ap[:, 0:L], self.slot_tile(sl, m * 128),
                                self.inter[:, kc * W: kc * W + L],
                                start=(kc == 0), stop=(kc == IC - 1))
            self.sT.inc(inst)
            self.slot_release([sl])
            if kc + 4 < IC:
                self.slot_load([sl], SC_CMV + (kc + 4) * 1024, 1024,
                               self.pre["L0_attn_cm_Wv"])
        self.sT.inc(inst)
        for bk in self.ps:
            bk.wn = self.sT.n
        self.read_banks(V, self.ps)
        for m in range(FC):
            V.tensor_tensor(out=self.tmp1[:, 0:L], in0=self.ps[m].ap[:, 0:L],
                            in1=self.gate[:, m * W: m * W + L], op=ALU.mult)
            inst = V.tensor_tensor(out=self.fcs(self.x, m, L), in0=self.tmp1[:, 0:L],
                                   in1=self.fcs(self.x, m, L), op=ALU.add)
        self.sV.inc(inst)
        self.reads_done(self.ps, self.sV)
        if self.debug and c == 0:
            db = self.dbgs
            V.tensor_copy(out=db[:, 0:FC], in_=self.Sp)
            V.tensor_copy(out=db[:, FC:2 * FC], in_=self.Sq)
            V.tensor_copy(out=db[:, 2 * FC:3 * FC], in_=self.hc_tm)
            inst = V.tensor_copy(out=db[:, 3 * FC:4 * FC], in_=self.hc_cm)
            self.sV.inc(inst)

    def load_act_set(self, set_id):
        return self.S.add_instruction(
            mybir.InstLoadActFuncSet(
                name=self.nc.get_next_instruction_name(),
                act_func_set_id=set_id, ins=[], outs=[]))

    def dpow(self, j, s):
        return self.dpows[:, s * FC + j: s * FC + j + 1]

    def prefix(self, src, buf_a, buf_b, L, j):
        V = self.V
        off, s = 1, 0
        V.tensor_copy(out=buf_a[:, 0:off], in_=src[:, 0:off])
        V.scalar_tensor_tensor(out=buf_a[:, off:L], in0=src[:, 0:L - off],
                               scalar=self.dpow(j, s), in1=src[:, off:L],
                               op0=ALU.mult, op1=ALU.add)
        cur, oth = buf_a, buf_b
        off, s = 2, 1
        while off < L:
            V.drain()
            V.tensor_copy(out=oth[:, 0:off], in_=cur[:, 0:off])
            V.scalar_tensor_tensor(out=oth[:, off:L], in0=cur[:, 0:L - off],
                                   scalar=self.dpow(j, s), in1=cur[:, off:L],
                                   op0=ALU.mult, op1=ALU.add)
            cur, oth = oth, cur
            off, s = off * 2, s + 1
        return cur, oth

    def dbg_dump(self, tag, col, src_buf, L, nfc=FC):
        if not self.debug:
            return
        V = self.V
        inst = V.tensor_copy(
            out=self.dbgs.rearrange("p (f o) -> p f o", o=1)[:, col:col + nfc, :],
            in_=src_buf.rearrange("p (f w) -> p f w", f=nfc)[:, :, L - 1:L])
        self.sV.inc(inst)

    def chunk(self, c, L, t0):
        Yq, V = self.Y, self.V
        self.sO.wait(Yq)
        self.sV.wait(Yq)
        src = self.x_fm[:, self.chk_off(c): self.chk_off(c) + L]
        src = src.rearrange("(f p) t -> p f t", p=128)
        self.sH.inc(Yq.dma_start(out=self.view3(self.x, L), in_=src), 16)
        self.sH.wait(V)
        if c == 0 and self.debug:
            self.dbg_dump("xin", 32, self.x, L)
        if not self.skip_ffn:
            self.ffn(c, L)
        if c == 0 and self.debug:
            self.dbg_dump("x1", 40, self.x, L)
        if not self.skip_attn:
            self.attn(c, L)
        if not self.skip_ffn:
            self.ffn(c, L)
        if c >= 1:
            dst = self.o_fm[:, t0:t0 + L].rearrange("(f p) t -> p f t", p=128)
            self.sV.wait(Yq)
            self.sO.inc(Yq.dma_start(out=dst, in_=self.view3(self.x, L)), 16)

    def chk_off(self, c):
        return sum(self.chunks[:c])

    def emit(self):
        self.init_phase()
        self.prepass()
        t0 = 0
        for c, L in enumerate(self.chunks):
            self.chunk(c, L, t0)
            if c >= 1:
                t0 += L
        if self.debug:
            self.sV.wait(self.Y)
            self.sO.inc(self.Y.dma_start(out=self.dbg, in_=self.dbgs), 16)
        self.Y.wait_ge(self.sO.h, self.sO.n)

    def build(self):
        self.declare()
        with self.ctx:
            with self.nc.Block() as blk:
                @blk.sync
                def _(_):
                    self.emit()
        return self.nc


WNAMES = ["L0_ffn_W1", "L0_ffn_Wo", "L0_attn_tm_Wk", "L0_attn_tm_Wv", "L0_attn_tm_Wr",
          "L0_attn_Wo", "L0_attn_cm_Wr", "L0_attn_cm_Wk", "L0_attn_cm_Wv"]
_VEC_IN = ["L0_attn_tm_mix_k", "L0_attn_tm_mix_v", "L0_attn_tm_mix_r",
           "L0_attn_time_decay", "L0_attn_time_first", "L0_attn_cm_mix_k",
           "L0_attn_cm_mix_r", "L0_ffn_bo", "L0_attn_ln1_g", "L0_attn_ln1_b",
           "L0_attn_ln2_g", "L0_attn_ln2_b", "L0_ffn_ln_gamma", "L0_ffn_ln_beta"]


def pack_vecs(inp):
    cols = [np.asarray(inp[k], np.float32).reshape(FC, 128).T for k in _VEC_IN]
    return np.ascontiguousarray(np.concatenate(cols, axis=1))


def kernel(**inp):
    x = np.asarray(inp["L0_attn_inputs"], np.float32)
    Bb, Tt, _ = x.shape
    half = Tt // 2
    chunks = [HALO] + [W] * (half // W)
    bld = B(chunks)
    nc = bld.build()
    vecs = pack_vecs(inp)
    wmats = {k: np.ascontiguousarray(np.asarray(inp[k], np.float32)) for k in WNAMES}
    wr = np.ascontiguousarray(np.asarray(inp["L0_ffn_W_router"], np.float32))
    b1 = np.ascontiguousarray(np.asarray(inp["L0_ffn_b1"], np.float32))
    in_maps = []
    for core in range(NCORE):
        b, hh = core // 2, core % 2
        s = hh * half
        if hh == 0:
            xs = np.concatenate([x[b, 0:HALO], x[b, 0:half]], axis=0)
            mrow = np.zeros((1, HALO), np.float32)
            v128 = np.zeros((128, 1), np.float32)
        else:
            xs = x[b, s - HALO:s + half]
            mrow = np.ones((1, HALO), np.float32)
            v128 = np.ones((128, 1), np.float32)
        m = dict(x_fm=np.ascontiguousarray(xs.T), mrow=mrow, v128=v128, vecs=vecs,
                 L0_ffn_W_router=wr, L0_ffn_b1=b1)
        m.update(wmats)
        in_maps.append(m)
    res = run_bass_kernel_spmd(nc, in_maps, core_ids=list(range(NCORE)))
    out = np.empty((Bb, Tt, E), np.float32)
    for core, r in enumerate(res.results):
        b, hh = core // 2, core % 2
        out[b, hh * half:(hh + 1) * half] = r["o_fm"].T
    return out
